# version 07
# baseline (speedup 1.0000x reference)
import numpy as np
from contextlib import ExitStack

import concourse.bass as bass
import concourse.mybir as mybir
from concourse.bass_utils import run_bass_kernel_spmd
from concourse.alu_op_type import AluOpType as OP

F32 = mybir.dt.float32
BF16 = mybir.dt.bfloat16
I32 = mybir.dt.int32
AF = mybir.ActivationFunctionType
AX = mybir.AxisListType

NCORES = 8
S = 4096
D = 2048
T = 512
NT = S // T
DC = D // 128
HID = 8192
MC = HID // 128
NH = 16
EPS_LN = 1e-5
EPS_ATTN = 1e-6

ENGINES = ("sync", "tensor", "vector", "scalar", "gpsimd")


class Sched:
    def __init__(self, sems):
        self.q = {e: [] for e in ENGINES}
        self.cnt = {k: 0 for k in sems}
        self.sems = sems

    def op(self, eng, fn, waits=(), inc=None, amt=1):
        ws = tuple((self.sems[s] if isinstance(s, str) else s, v)
                   for (s, v) in waits if v > 0)
        self.q[eng].append((ws, fn, self.sems[inc] if inc else None, amt))
        if inc is not None:
            self.cnt[inc] += amt
            return self.cnt[inc]
        return None

    def at(self, sem):
        return self.cnt[sem]


class Rot:

    def __init__(self, n):
        self.pend = [[] for _ in range(n)]

    def wait(self, i):
        sl = self.pend[i % len(self.pend)]
        out = list(sl)
        sl.clear()
        return out

    def done(self, i, sem, cnt):
        self.pend[i % len(self.pend)].append((sem, cnt))


def build_nc(debug=False):
    nc = bass.Bass("TRN2", target_bir_lowering=False)

    xT = nc.dram_tensor("xT", [D, S], F32, kind="ExternalInput").ap()
    wqT = nc.dram_tensor("wqT", [D, D], F32, kind="ExternalInput").ap()
    wkT = nc.dram_tensor("wkT", [D, 512], F32, kind="ExternalInput").ap()
    wvT = nc.dram_tensor("wvT", [D, 512], F32, kind="ExternalInput").ap()
    woT = nc.dram_tensor("woT", [D, D], F32, kind="ExternalInput").ap()
    w1T = nc.dram_tensor("w1T", [D, HID], F32, kind="ExternalInput").ap()
    w2T = nc.dram_tensor("w2T", [HID, D], F32, kind="ExternalInput").ap()
    bq_r = nc.dram_tensor("bq_r", [128, DC], F32, kind="ExternalInput").ap()
    bk_r = nc.dram_tensor("bk_r", [1, 512], F32, kind="ExternalInput").ap()
    bv_r = nc.dram_tensor("bv_r", [1, 512], F32, kind="ExternalInput").ap()
    b1_r = nc.dram_tensor("b1_r", [128, MC], F32, kind="ExternalInput").ap()
    b2_r = nc.dram_tensor("b2_r", [128, DC], F32, kind="ExternalInput").ap()
    g_r = nc.dram_tensor("g_r", [128, MC], F32, kind="ExternalInput").ap()
    bb_r = nc.dram_tensor("bb_r", [128, MC], F32, kind="ExternalInput").ap()
    identb_d = nc.dram_tensor("identb", [128, 128], BF16, kind="ExternalInput").ap()
    identf_d = nc.dram_tensor("identf", [128, 128], F32, kind="ExternalInput").ap()
    outT = nc.dram_tensor("outT", [D, S], F32, kind="ExternalOutput").ap()

    dbg = "ExternalOutput" if debug else "Internal"
    if debug:
        dbgm = nc.dram_tensor("dbgm", [128, 16], F32, kind="ExternalOutput").ap()
        dbgz = nc.dram_tensor("dbgz", [128, DC * T], BF16, kind="ExternalOutput").ap()
        dbgh = nc.dram_tensor("dbgh", [128, MC * T], BF16, kind="ExternalOutput").ap()
        dbgl = nc.dram_tensor("dbgl", [128, MC * T], BF16, kind="ExternalOutput").ap()
        dbgq = nc.dram_tensor("dbgq", [128, MC * T], BF16, kind="ExternalOutput").ap()
        dbga = nc.dram_tensor("dbga", [128, DC * T], BF16, kind="ExternalOutput").ap()
    sqTd = nc.dram_tensor("sqTd", [D, S], BF16, kind=dbg).ap()
    xnTd = nc.dram_tensor("xnTd", [D, S], F32, kind=dbg).ap()
    wq1d = nc.dram_tensor("wq1d", [D, HID], BF16, kind=dbg).ap()
    wq2d = nc.dram_tensor("wq2d", [DC, HID, 128], BF16, kind=dbg).ap()

    es = ExitStack()
    SB = lambda name, shape, dt: es.enter_context(nc.sbuf_tensor(name, shape, dt))
    PSm = lambda name, shape, dt: es.enter_context(nc.psum_tensor(name, shape, dt))

    with es:
        A1 = SB("A1", [128, 32768], BF16)
        A2 = SB("A2", [128, 16384], BF16)
        xf = SB("xf", [128, DC * T], F32)
        yb = SB("yb", [128, DC * T], BF16)
        sqs = SB("sqs", [128, DC * T], BF16)
        xb1 = SB("xb1", [128, DC * T], BF16)
        tmpA = SB("tmpA", [128, 1024], F32)
        tmpB = SB("tmpB", [128, 512], F32)
        bcA = SB("bcA", [128, 512], F32)
        bcB = SB("bcB", [128, 512], F32)
        bcAb = SB("bcAb", [128, 512], BF16)
        aux32 = SB("aux32", [128, 1024], I32)
        rows = SB("rows", [1, 2560], F32)
        bro = SB("bro", [1, 1024], BF16)
        kvb = SB("kvb", [128, 520], BF16)
        bqc = SB("bqc", [128, DC], F32)
        b2c = SB("b2c", [128, DC], F32)
        b1c = SB("b1c", [128, MC], F32)
        gc = SB("gc", [128, MC], F32)
        bbc = SB("bbc", [128, MC], F32)
        onesb = SB("onesb", [128, 4], BF16)
        onesrf = SB("onesrf", [1, 128], F32)
        onesrb = SB("onesrb", [1, 128], BF16)
        epsc = SB("epsc", [128, 4], F32)
        swc = SB("swc", [128, 4], F32)
        wrow = SB("wrow", [1, 8], F32)
        wacc = SB("wacc", [128, 4], F32)
        tcol = SB("tcol", [128, 16], F32)
        idb = SB("idb", [128, 128], BF16)
        idf = SB("idf", [128, 128], F32)

        pmm0 = PSm("pmm0", [128, 512], F32)
        pmm1 = PSm("pmm1", [128, 512], F32)
        pst0 = PSm("pst0", [128, 512], F32)
        pst1 = PSm("pst1", [128, 512], F32)
        pkvA = PSm("pkvA", [128, 512], F32)
        pkvB = PSm("pkvB", [128, 16], F32)
        ptb = PSm("ptb", [128, 1024], BF16)
        psp = PSm("psp", [128, 512], F32)

        semnames = ("ld", "st", "stg", "pe", "v", "s", "p")
        sems = {n: es.enter_context(nc.semaphore(n)) for n in semnames}
        hs = sems
        sch = Sched(hs)
        W = lambda s, v: (hs[s], v)

        QSEM = {"sync": "ld", "scalar": "st", "gpsimd": "stg"}

        def d3(sb):
            pass

        def dma(queue, out, in_, waits=()):
            sem = QSEM[queue]
            return sch.op(queue, lambda e: e.dma_start(out=out, in_=in_),
                          waits=waits, inc=sem, amt=16)

        def mm(out, lhsT, rhs, start, stop, waits=(), inc=None):
            return sch.op("tensor",
                          lambda e: e.matmul(out=out, lhsT=lhsT, rhs=rhs,
                                             start=start, stop=stop,
                                             skip_group_check=True),
                          waits=waits, inc=inc)

        def tpose(out, in_, ident, waits=(), inc=None):
            return sch.op("tensor",
                          lambda e: e.transpose(out=out, in_=in_, identity=ident),
                          waits=waits, inc=inc)

        vo = lambda fn, waits=(), inc=None: sch.op("vector", fn, waits, inc)
        so = lambda fn, waits=(), inc=None: sch.op("scalar", fn, waits, inc)
        po = lambda fn, waits=(), inc=None: sch.op("gpsimd", fn, waits, inc)

        def vs(fn, waits=()):
            return vo(fn, waits=list(waits) + [W("v", sch.at("v"))], inc="v")

        def ss(fn, waits=()):
            return so(fn, waits=list(waits) + [W("s", sch.at("s"))], inc="s")

        pmm = (pmm0, pmm1)
        pstb = (pst0, pst1)
        R = lambda i: rows[0:1, i * 512:(i + 1) * 512]
        tA = lambda i: tmpA[:, (i % 2) * 512:(i % 2) * 512 + 512]
        tB = lambda i: tmpB[:, 0:512]
        aux = lambda i: aux32[:, (i % 2) * 512:(i % 2) * 512 + 512]
        chunk = lambda t, c, w=512: t[:, c * w:(c + 1) * w]

        bankw = {id(b): [] for b in (pmm0, pmm1, pst0, pst1, psp, ptb)}

        def guard(b):
            wl = bankw[id(b)]
            out = list(wl)
            wl.clear()
            return out

        consumed = lambda b, s, v: bankw[id(b)].append(W(s, v))
        rot_bro = Rot(1)

        def ln_rows(pe_done, n):
            sA = ss(lambda e: e.activation(out=R(0), in_=pst0[0:1, :],
                                           func=AF.Identity, scale=1.0 / n),
                    waits=[W("pe", pe_done)])
            sB = ss(lambda e: e.activation(out=R(1), in_=pst1[0:1, :],
                                           func=AF.Identity, scale=1.0 / n))
            consumed(pst0, "s", sA)
            consumed(pst1, "s", sB)
            vs(lambda e: e.tensor_tensor(out=R(2), in0=R(0), in1=R(0), op=OP.mult),
               waits=[W("s", sB)])
            v1 = vs(lambda e: e.tensor_tensor(out=R(1), in0=R(1), in1=R(2),
                                              op=OP.subtract))
            ss(lambda e: e.activation(out=R(2), in_=R(1), func=AF.Ln,
                                      bias=epsc[0:1, 0:1]),
               waits=[W("v", v1)])
            sC = ss(lambda e: e.activation(out=R(2), in_=R(2), func=AF.Exp,
                                           scale=-0.5))
            v2 = vs(lambda e: e.tensor_tensor(out=R(3), in0=R(0), in1=R(2),
                                              op=OP.mult),
                    waits=[W("s", sC)])
            return sC, v2

        def bcast(rowslice, bank, dest, bf16, row_w=()):
            if bf16:
                vx = vs(lambda e: e.tensor_copy(out=bro[0:1, 0:512],
                                                in_=rowslice),
                        waits=list(row_w) + rot_bro.wait(0))
                t = mm(bank[:, 0:512], onesrb[0:1, :], bro[0:1, 0:512],
                       True, True, waits=[W("v", vx)] + guard(bank), inc="pe")
                rot_bro.done(0, "pe", t)
            else:
                t = mm(bank[:, 0:512], onesrf[0:1, :], rowslice, True, True,
                       waits=list(row_w) + guard(bank), inc="pe")
            sE = so(lambda e: e.activation(out=dest, in_=bank[:, 0:512],
                                           func=AF.Identity),
                    waits=[W("pe", t)], inc="s")
            consumed(bank, "s", sE)
            return sE

        vo(lambda e: e.memset(onesb[:], 1.0))
        vo(lambda e: e.memset(onesrf[:], 1.0))
        vo(lambda e: e.memset(onesrb[:], 1.0))
        vo(lambda e: e.memset(epsc[:, 0:1], EPS_LN))
        vo(lambda e: e.memset(epsc[:, 1:2], EPS_ATTN))
        vo(lambda e: e.memset(epsc[:, 2:3], 1e-30))
        vo(lambda e: e.memset(wacc[:], 0.0))
        dma("sync", idb[:], identb_d)
        dma("sync", idf[:], identf_d)
        dma("sync", bqc[:], bq_r)
        dma("sync", b2c[:], b2_r)
        dma("sync", b1c[:], b1_r)
        dma("sync", gc[:], g_r)
        dma("sync", bbc[:], bb_r)
        dma("sync", tmpA[0:1, 0:512], bk_r)
        c0 = dma("sync", tmpB[0:1, 0:512], bv_r)
        vo(lambda e: e.tensor_copy(out=bro[0:1, 0:512], in_=tmpA[0:1, 0:512]),
           waits=[W("ld", c0)])
        vlast = vo(lambda e: e.tensor_copy(out=bro[0:1, 512:1024],
                                           in_=tmpB[0:1, 0:512]), inc="v")

        c_wk = dma("sync", xf[:].rearrange("p (c o) -> p c o", c=16), wkT.rearrange("(c p) o -> p c o", p=128),
                   waits=[W("v", vlast)])
        for c in range(DC):
            vlast = vo(lambda e, c=c: e.tensor_copy(out=chunk(A2, c),
                                                    in_=chunk(xf, c)),
                       waits=[W("ld", c_wk)] if c == 0 else (),
                       inc="v" if c == DC - 1 else None)
        c_wv = dma("sync", xf[:].rearrange("p (c o) -> p c o", c=16), wvT.rearrange("(c p) o -> p c o", p=128),
                   waits=[W("v", vlast)])
        for c in range(DC):
            vlast = vo(lambda e, c=c: e.tensor_copy(
                out=A2[:, 8192 + c * 512:8192 + (c + 1) * 512], in_=chunk(xf, c)),
                waits=[W("ld", c_wv)] if c == 0 else (),
                inc="v" if c == DC - 1 else None)
        slast = sch.at("s")
        for q4 in range(4):
            src = wqT[q4 * 512:(q4 + 1) * 512, :].rearrange("(c p) o -> p c o",
                                                            p=128)
            c_l = dma("sync", xf[:].rearrange("p (c o) -> p c o", c=4), src, waits=[W("v", vlast), W("s", slast)])
            for c in range(4):
                dc = q4 * 4 + c
                if c % 2 == 0:
                    vlast = vo(lambda e, dc=dc, c=c: e.tensor_copy(
                        out=A1[:, dc * 2048:(dc + 1) * 2048],
                        in_=xf[:, c * 2048:(c + 1) * 2048]),
                        waits=[W("ld", c_l)], inc="v")
                else:
                    slast = so(lambda e, dc=dc, c=c: e.copy(
                        out=A1[:, dc * 2048:(dc + 1) * 2048],
                        in_=xf[:, c * 2048:(c + 1) * 2048]),
                        waits=[W("ld", c_l)], inc="s")

        rot_auxp = Rot(2)
        for wi, wsrc in enumerate((w1T, w2T)):
            nch = DC if wi == 0 else MC
            wid = HID if wi == 0 else D
            for ch in range(nch):
                c_l = dma("sync", xf[:, 0:wid], wsrc[ch * 128:(ch + 1) * 128, :],
                          waits=[W("v", sch.at("v")), W("s", sch.at("s"))])
                vs(lambda e, wid=wid: e.reduce_sum(
                    out=wacc[:, 2:3], in_=xf[:, 0:wid], axis=AX.X,
                    apply_absolute_value=True), waits=[W("ld", c_l)])
                vs(lambda e, wi=wi: e.tensor_tensor(
                    out=wacc[:, wi:wi + 1], in0=wacc[:, wi:wi + 1],
                    in1=wacc[:, 2:3], op=OP.add))
            t_tp = tpose(psp[0:1, 0:128], wacc[:, wi:wi + 1], idf[:],
                         waits=[W("v", sch.at("v"))] + guard(psp), inc="pe")
            vs(lambda e: e.tensor_reduce(out=wrow[0:1, 0:1], in_=psp[0:1, 0:128],
                                         axis=AX.X, op=OP.add),
               waits=[W("pe", t_tp)])
            vs(lambda e: e.tensor_scalar(out=wrow[0:1, 1:2], in0=wrow[0:1, 0:1],
                                         scalar1=1.0 / float(D * HID), scalar2=1e-5,
                                         op0=OP.mult, op1=OP.max))
            vs(lambda e: e.reciprocal(out=wrow[0:1, 2:3], in_=wrow[0:1, 1:2]))
            v_sw = vs(lambda e, wi=wi: e.tensor_scalar(
                out=wrow[0:1, 4 + wi:5 + wi], in0=wrow[0:1, 1:2],
                scalar1=1.0 / 127.0, scalar2=None, op0=OP.mult))
            consumed(psp, "v", v_sw)
            t_bc = mm(psp[:, 0:1], onesrf[0:1, :], wrow[0:1, 2:3], True, True,
                      waits=[W("v", v_sw)] + guard(psp), inc="pe")
            v_swc = vs(lambda e, wi=wi: e.tensor_copy(out=swc[:, wi:wi + 1],
                                                      in_=psp[:, 0:1]),
                       waits=[W("pe", t_bc)])
            consumed(psp, "v", v_swc)
            for ch in range(nch):
                c_l = dma("sync", xf[:, 0:wid], wsrc[ch * 128:(ch + 1) * 128, :],
                          waits=[W("v", sch.at("v")), W("p", sch.at("p")),
                                 W("st", sch.at("st")), W("stg", sch.at("stg"))])
                for j in range(wid // 512):
                    vo(lambda e, j=j, wi=wi: e.tensor_scalar(
                        out=aux(j), in0=chunk(xf, j), scalar1=swc[:, wi:wi + 1],
                        scalar2=None, op0=OP.mult),
                       waits=([W("ld", c_l)] if j == 0 else []) + rot_auxp.wait(j),
                       inc="v")
                    fn = lambda e, j=j: e.tensor_scalar(
                        out=chunk(yb, j), in0=aux(j), scalar1=-1.0, scalar2=1.0,
                        op0=OP.max, op1=OP.min)
                    if j % 2 == 0:
                        pc = po(fn, waits=[W("v", sch.at("v"))], inc="p")
                        rot_auxp.done(j, "p", pc)
                    else:
                        vc = vo(fn, inc="v")
                        rot_auxp.done(j, "v", vc)
                m_v, m_p = sch.at("v"), sch.at("p")
                if wi == 0:
                    dma("scalar", wq1d[ch * 128:(ch + 1) * 128, :], yb[:, 0:HID],
                        waits=[W("v", m_v), W("p", m_p)])
                else:
                    for jc in range(DC):
                        q = "scalar" if jc % 2 == 0 else "gpsimd"
                        dma(q, wq2d[jc, ch * 128:(ch + 1) * 128, :],
                            yb[:, jc * 128:(jc + 1) * 128],
                            waits=[W("v", m_v), W("p", m_p)] if jc < 2 else ())
        if debug:
            v_dbg = vo(lambda e: e.tensor_copy(out=tcol[:, 0:4], in_=wacc[:]),
                       waits=[W("v", sch.at("v"))], inc="v")
            v_dbg = vo(lambda e: e.tensor_copy(out=tcol[:, 4:8], in_=swc[:]),
                       inc="v")
            v_dbg = vo(lambda e: e.tensor_copy(out=tcol[0:1, 8:16],
                                               in_=wrow[0:1, 0:8]), inc="v")
            dma("gpsimd", dbgm[:], tcol[:, 0:16], waits=[W("v", v_dbg)])
        pre_v, pre_s, pre_p = sch.at("v"), sch.at("s"), sch.at("p")
        pre_st, pre_stg = sch.at("st"), sch.at("stg")

        xb_sl = lambda i: xb1[:, (i % 2) * 512:(i % 2) * 512 + 512]
        sq_sl = lambda i: xb1[:, 1024 + (i % 2) * 512:1024 + (i % 2) * 512 + 512]
        sks = lambda s_, h: xb1[:, 2048 + s_ * 512 + h * 128:
                                2048 + s_ * 512 + (h + 1) * 128]
        vbs = lambda s_, h: xb1[:, 4096 + s_ * 512 + h * 128:
                                4096 + s_ * 512 + (h + 1) * 128]
        sksT = lambda s_: xb1[:, 2048 + s_ * 512:2048 + (s_ + 1) * 512]
        vbsT = lambda s_: xb1[:, 4096 + s_ * 512:4096 + (s_ + 1) * 512]

        rot_xb = Rot(2)
        rot_sq = Rot(2)
        rot_tA = Rot(2)
        rot_tB = Rot(2)
        rot_aux = Rot(2)
        rot_aux.done(0, "p", pre_p)
        rot_aux.done(1, "p", pre_p)
        rot_aux.done(0, "v", pre_v)
        rot_aux.done(1, "v", pre_v)
        xf_free = [W("v", pre_v), W("s", pre_s), W("p", pre_p)]
        yb_free = [W("st", pre_st), W("stg", pre_stg)]
        sqs_free = []
        kv_started = False
        last_kv_pe = 0

        for i in range(NT):
            c_x = dma("sync", xf[:].rearrange("p (c t) -> p c t", c=DC),
                      xT[:, i * T:(i + 1) * T].rearrange("(c p) t -> p c t", p=128),
                      waits=xf_free)
            pe_sq = 0
            for c in range(DC):
                v_xb = vo(lambda e, c=c: e.tensor_copy(out=xb_sl(c),
                                                       in_=chunk(xf, c)),
                          waits=([W("ld", c_x)] if c == 0 else []) + rot_xb.wait(c),
                          inc="v")
                s_sq = so(lambda e, c=c: e.activation(out=sq_sl(c), in_=xb_sl(c),
                                                      func=AF.Square),
                          waits=[W("v", v_xb)] + rot_sq.wait(c), inc="s")
                mm(pst0[0:1, :], onesb[:, 0:1], xb_sl(c), c == 0, c == DC - 1,
                   waits=[W("v", v_xb)] + (guard(pst0) if c == 0 else []))
                pe_sq = mm(pst1[0:1, :], onesb[:, 0:1], sq_sl(c), c == 0, c == DC - 1,
                           waits=[W("s", s_sq)] + (guard(pst1) if c == 0 else []),
                           inc="pe")
                rot_xb.done(c, "pe", pe_sq)
                rot_sq.done(c, "pe", pe_sq)
            s_ln, v_mub = ln_rows(pe_sq, D)
            s_bA = bcast(R(2), pst0, bcA[:, :], False, row_w=[W("v", v_mub)])
            s_bB = bcast(R(3), pst1, bcB[:, :], False, row_w=[W("v", v_mub)])
            v_c = p_c = 0
            for c in range(DC):
                v_t1 = vo(lambda e, c=c: e.scalar_tensor_tensor(
                    out=tA(c), in0=chunk(xf, c), scalar=1.0, in1=bcA[:, :],
                    op0=OP.bypass, op1=OP.mult),
                    waits=([W("s", s_bA)] if c == 0 else []) + rot_tA.wait(c) +
                          (yb_free if c == 0 else []), inc="v")
                fn = lambda e, c=c: e.tensor_tensor(
                    out=chunk(yb, c), in0=tA(c), in1=bcB[:, :], op=OP.subtract)
                if c % 2 == 0:
                    p_c = po(fn, waits=[W("v", v_t1)] +
                             ([W("s", s_bB)] if c == 0 else []), inc="p")
                    rot_tA.done(c, "p", p_c)
                else:
                    v_c = vo(fn, waits=[W("s", s_bB)] if c == 1 else (), inc="v")
                    rot_tA.done(c, "v", v_c)
            xf_free = [W("v", sch.at("v")), W("p", p_c)]
            m_yb = [W("v", v_c), W("p", p_c)]
            for s_ in range(4):
                g = s_ % 2
                for c in range(DC):
                    mm(pmm[g][:, 0:512],
                       yb[:, c * 512 + s_ * 128:c * 512 + s_ * 128 + 128],
                       chunk(A2, c), c == 0, False,
                       waits=(m_yb if s_ == 0 and c == 0 else []) +
                             (guard(pmm[g]) if c == 0 else []))
                pe_k = mm(pmm[g][:, 0:512], onesrb[0:1, :], bro[0:1, 0:512], False, True,
                          inc="pe")
                v_m = vo(lambda e, g=g: e.tensor_scalar(
                    out=tA(0), in0=pmm[g][:, 0:512], scalar1=0.0, scalar2=None,
                    op0=OP.min), waits=[W("pe", pe_k)] + rot_tA.wait(0), inc="v")
                s_e = so(lambda e: e.activation(out=tB(0), in_=tA(0), func=AF.Exp),
                         waits=[W("v", v_m)] + rot_tB.wait(0), inc="s")
                v_r = vo(lambda e, g=g: e.tensor_scalar(
                    out=tA(1), in0=pmm[g][:, 0:512], scalar1=0.0, scalar2=None,
                    op0=OP.max), waits=rot_tA.wait(1), inc="v")
                v_sk = vo(lambda e, s_=s_: e.tensor_tensor(
                    out=sksT(s_), in0=tB(0), in1=tA(1), op=OP.add),
                    waits=[W("s", s_e)], inc="v")
                rot_tA.done(0, "s", s_e)
                rot_tA.done(1, "v", v_sk)
                rot_tB.done(0, "v", v_sk)
                consumed(pmm[g], "v", v_r)
                for c in range(DC):
                    mm(pmm[1 - g][:, 0:512],
                       yb[:, c * 512 + s_ * 128:c * 512 + s_ * 128 + 128],
                       A2[:, 8192 + c * 512:8192 + (c + 1) * 512], c == 0, False,
                       waits=(guard(pmm[1 - g]) if c == 0 else []))
                pe_v = mm(pmm[1 - g][:, 0:512], onesrb[0:1, :], bro[0:1, 512:1024], False,
                          True, inc="pe")
                v_vb = vo(lambda e, s_=s_, g=g: e.tensor_copy(
                    out=vbsT(s_), in_=pmm[1 - g][:, 0:512]),
                    waits=[W("pe", pe_v)], inc="v")
                consumed(pmm[1 - g], "v", v_vb)
                first = not kv_started
                for h in range(4):
                    mm(pkvA[:, h * 128:(h + 1) * 128], sks(s_, h), vbs(s_, h),
                       first and h == 0, False,
                       waits=[W("v", v_vb)] if h == 0 else [])
                for h in range(4):
                    last_kv_pe = mm(pkvB[:, h:h + 1], sks(s_, h), onesb[:, 0:1],
                                    first and h == 0, False,
                                    inc="pe" if h == 3 else None)
                kv_started = True
            v_sq2 = 0
            for oc in range(DC):
                g = oc % 2
                pe_q = None
                for c in range(DC):
                    pe_q = mm(pmm[g][:, 0:512],
                              A1[:, c * 2048 + oc * 128:c * 2048 + oc * 128 + 128],
                              chunk(yb, c), c == 0, c == DC - 1,
                              waits=(guard(pmm[g]) if c == 0 else []),
                              inc="pe" if c == DC - 1 else None)
                v_m = vo(lambda e, g=g, oc=oc: e.tensor_scalar(
                    out=tA(0), in0=pmm[g][:, 0:512], scalar1=bqc[:, oc:oc + 1],
                    scalar2=0.0, op0=OP.add, op1=OP.min),
                    waits=[W("pe", pe_q)] + rot_tA.wait(0), inc="v")
                s_e = so(lambda e: e.activation(out=tB(0), in_=tA(0), func=AF.Exp),
                         waits=[W("v", v_m)] + rot_tB.wait(0), inc="s")
                v_r = vo(lambda e, g=g, oc=oc: e.tensor_scalar(
                    out=tA(1), in0=pmm[g][:, 0:512], scalar1=bqc[:, oc:oc + 1],
                    scalar2=0.0, op0=OP.add, op1=OP.max),
                    waits=rot_tA.wait(1), inc="v")
                v_sq2 = vo(lambda e, oc=oc: e.tensor_tensor(
                    out=chunk(sqs, oc), in0=tB(0), in1=tA(1), op=OP.add),
                    waits=[W("s", s_e)] + (sqs_free if oc == 0 else []), inc="v")
                rot_tA.done(0, "s", s_e)
                rot_tA.done(1, "v", v_sq2)
                rot_tB.done(0, "v", v_sq2)
                consumed(pmm[g], "v", v_r)
            yb_free = [W("pe", sch.at("pe"))]
            st_sq = dma("scalar",
                        sqTd[:, i * T:(i + 1) * T].rearrange("(c p) t -> p c t",
                                                             p=128),
                        sqs[:].rearrange("p (c t) -> p c t", c=DC),
                        waits=[W("v", v_sq2)])
            sqs_free = [W("st", st_sq)]

        s_kv = so(lambda e: e.activation(out=kvb[:, 0:512], in_=pkvA[:, 0:512],
                                         func=AF.Identity),
                  waits=[W("pe", last_kv_pe)], inc="s")
        s_kv2 = so(lambda e: e.activation(out=kvb[:, 512:516], in_=pkvB[:, 0:4],
                                          func=AF.Identity), inc="s")

        vlast, slast = sch.at("v"), s_kv2
        pe_a1_free = sch.at("pe")
        for q4 in range(4):
            src = woT[q4 * 512:(q4 + 1) * 512, :].rearrange("(c p) o -> p c o",
                                                            p=128)
            c_l = dma("sync", xf[:].rearrange("p (c o) -> p c o", c=4),
                      src, waits=(xf_free if q4 == 0 else []) +
                      [W("v", vlast), W("s", slast)])
            for c in range(4):
                dc = q4 * 4 + c
                if c % 2 == 0:
                    vlast = vo(lambda e, dc=dc, c=c: e.tensor_copy(
                        out=A1[:, dc * 2048:(dc + 1) * 2048],
                        in_=xf[:, c * 2048:(c + 1) * 2048]),
                        waits=[W("ld", c_l)] +
                              ([W("pe", pe_a1_free)] if dc == 0 else []), inc="v")
                else:
                    slast = so(lambda e, dc=dc, c=c: e.copy(
                        out=A1[:, dc * 2048:(dc + 1) * 2048],
                        in_=xf[:, c * 2048:(c + 1) * 2048]),
                        waits=[W("ld", c_l)], inc="s")
        xf_free = [W("v", vlast), W("s", slast)]
        wo_ready = [W("v", vlast), W("s", slast)]

        rot_bc = Rot(2)
        rot_row = Rot(2)
        sqs_rd = sqs_free
        yb_free2 = yb_free
        xb1_free = []
        for i in range(NT):
            c_x = dma("sync", xf[:].rearrange("p (c t) -> p c t", c=DC),
                      xT[:, i * T:(i + 1) * T].rearrange("(c p) t -> p c t", p=128),
                      waits=xf_free)
            c_sq = dma("sync", sqs[:].rearrange("p (c t) -> p c t", c=DC),
                       sqTd[:, i * T:(i + 1) * T].rearrange("(c p) t -> p c t",
                                                            p=128),
                       waits=sqs_rd)
            bc2 = (bcA, bcB)
            v_at = 0
            for h in range(NH):
                g = h % 2
                hh = h // 4
                pe_qkv = mm(pmm[g][:, 0:512], kvb[:, hh * 128:(hh + 1) * 128], chunk(sqs, h),
                            True, True,
                            waits=([W("ld", c_sq)] + wo_ready if h == 0 else []) +
                                  guard(pmm[g]), inc="pe")
                pe_n = mm(pstb[g][0:1, :], kvb[:, 512 + hh:513 + hh], chunk(sqs, h),
                          True, True, waits=guard(pstb[g]), inc="pe")
                s_nr = so(lambda e, g=g: e.activation(
                    out=R(g), in_=pstb[g][0:1, :], func=AF.Identity,
                    bias=epsc[0:1, 1:2]),
                    waits=[W("pe", pe_n)] + rot_row.wait(h), inc="s")
                consumed(pstb[g], "s", s_nr)
                v_rc = vs(lambda e, g=g: e.reciprocal(out=R(g), in_=R(g)),
                          waits=[W("s", s_nr)])
                pe_bc = mm(pstb[g][:, 0:512], onesrf[0:1, :], R(g), True, True,
                           waits=[W("v", v_rc)] + guard(pstb[g]), inc="pe")
                rot_row.done(h, "pe", pe_bc)
                s_bc = so(lambda e, g=g, bc2=bc2: e.activation(
                    out=bc2[g][:, 0:512], in_=pstb[g][:, 0:512], func=AF.Identity),
                    waits=[W("pe", pe_bc)] + rot_bc.wait(h), inc="s")
                consumed(pstb[g], "s", s_bc)
                v_at = vo(lambda e, g=g, h=h, bc2=bc2: e.tensor_tensor(
                    out=chunk(yb, h), in0=pmm[g][:, 0:512], in1=bc2[g][:, 0:512],
                    op=OP.mult),
                    waits=[W("s", s_bc)] + (yb_free2 if h == 0 else []), inc="v")
                consumed(pmm[g], "v", v_at)
                rot_bc.done(h, "v", v_at)
            rot_sq2 = Rot(2)
            pe_sq2 = 0
            for jc in range(DC):
                g = jc % 2
                pw = None
                for hec in range(DC):
                    pw = mm(pmm[g][:, 0:512],
                            A1[:, hec * 2048 + jc * 128:hec * 2048 + jc * 128 + 128],
                            chunk(yb, hec), hec == 0, hec == DC - 1,
                            waits=([W("v", v_at)] if jc == 0 and hec == 0 else []) +
                                  (guard(pmm[g]) if hec == 0 else []),
                            inc="pe" if hec == DC - 1 else None)
                s_ev = so(lambda e, jc=jc, g=g: e.activation(
                    out=chunk(xb1, jc), in_=pmm[g][:, 0:512], func=AF.Identity),
                    waits=[W("pe", pw)] + (xb1_free if jc == 0 else []), inc="s")
                consumed(pmm[g], "s", s_ev)
                s_q = so(lambda e, jc=jc: e.activation(
                    out=sqs[:, (jc % 2) * 512:(jc % 2) * 512 + 512],
                    in_=chunk(xb1, jc), func=AF.Square),
                    waits=rot_sq2.wait(jc), inc="s")
                mm(pst0[0:1, :], onesb[:, 0:1], chunk(xb1, jc), jc == 0,
                   jc == DC - 1,
                   waits=[W("s", s_ev)] + (guard(pst0) if jc == 0 else []))
                pe_sq2 = mm(pst1[0:1, :], onesb[:, 0:1],
                            sqs[:, (jc % 2) * 512:(jc % 2) * 512 + 512],
                            jc == 0, jc == DC - 1,
                            waits=[W("s", s_q)] + (guard(pst1) if jc == 0 else []),
                            inc="pe")
                rot_sq2.done(jc, "pe", pe_sq2)
            s_ln2, v_mub2 = ln_rows(pe_sq2, D)
            s_bA2 = bcast(R(2), pst0, bcAb[:, :], True, row_w=[W("v", v_mub2)])
            s_bB2 = bcast(R(3), pst1, bcB[:, :], False, row_w=[W("v", v_mub2)])
            v_res = p_res = 0
            for c in range(DC):
                v_t1 = vo(lambda e, c=c: e.scalar_tensor_tensor(
                    out=tA(c), in0=chunk(xb1, c), scalar=1.0, in1=bcAb[:, :],
                    op0=OP.bypass, op1=OP.mult),
                    waits=([W("s", s_bA2)] if c == 0 else []) + rot_tA.wait(c),
                    inc="v")
                p_res = po(lambda e, c=c: e.tensor_tensor(
                    out=tB(c), in0=tA(c), in1=bcB[:, 0:512], op=OP.subtract),
                    waits=[W("v", v_t1)] + ([W("s", s_bB2)] if c == 0 else []) +
                          rot_tB.wait(c), inc="p")
                rot_tA.done(c, "p", p_res)
                v_res = vo(lambda e, c=c: e.tensor_tensor(
                    out=chunk(xf, c), in0=chunk(xf, c), in1=tB(c), op=OP.add),
                    waits=[W("p", p_res)], inc="v")
                rot_tB.done(c, "v", v_res)
            st_xn = dma("scalar",
                        xnTd[:, i * T:(i + 1) * T].rearrange("(c p) t -> p c t",
                                                             p=128),
                        xf[:].rearrange("p (c t) -> p c t", c=DC),
                        waits=[W("v", v_res)])
            xf_free = [W("st", st_xn)]
            sqs_rd = [W("pe", pe_sq2)]
            yb_free2 = [W("pe", sch.at("pe"))]
            xb1_free = [W("v", v_res), W("p", p_res)]

        Mpp = lambda i: sqs[:, 2048 + (i % 2) * 512:2048 + (i % 2) * 512 + 512]
        Npp = lambda i: sqs[:, 3072 + (i % 2) * 512:3072 + (i % 2) * 512 + 512]
        MQpp = lambda i: sqs[:, 2048 + (i % 2) * 512:2048 + (i % 2) * 512 + 512]
        zb_sl = lambda i: sqs[:, (i % 2) * 512:(i % 2) * 512 + 512]
        hq_sl = lambda i: sqs[:, 1024 + (i % 2) * 512:1024 + (i % 2) * 512 + 512]
        rot_w1 = Rot(2)
        rot_w2 = Rot(2)
        sqs_free3 = sqs_rd
        yb_free3 = yb_free2
        xb1_free3 = xb1_free

        def tree_max(buf, opx, idx, pre_w=()):
            tps = None
            for j in range(4):
                tps = tpose(ptb[:, idx * 512 + j * 128:idx * 512 + (j + 1) * 128],
                            buf[:, j * 128:(j + 1) * 128], idb[:],
                            waits=list(pre_w) + (guard(ptb) if j == 0 else []),
                            inc="pe")
            vr = 0
            for j in range(4):
                vr = vs(lambda e, j=j, idx=idx, opx=opx: e.tensor_reduce(
                    out=tcol[:, idx * 4 + j:idx * 4 + j + 1],
                    in_=ptb[:, idx * 512 + j * 128:idx * 512 + (j + 1) * 128],
                    axis=AX.X, op=opx), waits=[W("pe", tps)])
            consumed(ptb, "v", vr)
            return vr

        def tree_max_f32(buf, opx, idx, pre_w=()):
            tps = None
            for j in range(4):
                tps = tpose(psp[:, j * 128:(j + 1) * 128],
                            buf[:, j * 128:(j + 1) * 128], idf[:],
                            waits=list(pre_w) + (guard(psp) if j == 0 else []),
                            inc="pe")
            vr = 0
            for j in range(4):
                vr = vs(lambda e, j=j, idx=idx, opx=opx: e.tensor_reduce(
                    out=tcol[:, idx * 4 + j:idx * 4 + j + 1],
                    in_=psp[:, j * 128:(j + 1) * 128],
                    axis=AX.X, op=opx), waits=[W("pe", tps)])
            consumed(psp, "v", vr)
            return vr

        def cols_to_row(colidx, rslot, pre_w=()):
            tps = None
            for j in range(4):
                tps = tpose(psp[0:1, j * 128:(j + 1) * 128],
                            tcol[:, colidx + j:colidx + j + 1], idf[:],
                            waits=list(pre_w) + (guard(psp) if j == 0 else []),
                            inc="pe")
            vr = vs(lambda e, rslot=rslot: e.tensor_copy(out=R(rslot),
                                                         in_=psp[0:1, 0:512]),
                    waits=[W("pe", tps)])
            consumed(psp, "v", vr)
            return vr

        def row_to_cols(rslot, colidx, pre_w=()):
            tps = None
            for j in range(4):
                tps = tpose(psp[:, 480 + j:480 + j + 1],
                            R(rslot)[0:1, j * 128:(j + 1) * 128], idf[0:1, 0:1],
                            waits=list(pre_w) + (guard(psp) if j == 0 else []),
                            inc="pe")
            vr = vs(lambda e, colidx=colidx: e.tensor_copy(
                out=tcol[:, colidx:colidx + 4], in_=psp[:, 480:484]),
                waits=[W("pe", tps)])
            consumed(psp, "v", vr)
            return vr

        for i in range(NT):
            c_x = dma("sync", xf[:].rearrange("p (c t) -> p c t", c=DC),
                      xnTd[:, i * T:(i + 1) * T].rearrange("(c p) t -> p c t",
                                                           p=128),
                      waits=xf_free)
            pe_sq3 = 0
            v_M = p_N = 0
            for c in range(DC):
                v_xb = vo(lambda e, c=c: e.tensor_copy(out=zb_sl(c),
                                                       in_=chunk(xf, c)),
                          waits=([W("ld", c_x)] if c == 0 else []) +
                                rot_xb.wait(c) + (sqs_free3 if c == 0 else []),
                          inc="v")
                s_sq = so(lambda e, c=c: e.activation(out=hq_sl(c), in_=zb_sl(c),
                                                      func=AF.Square),
                          waits=[W("v", v_xb)] + rot_sq.wait(c), inc="s")
                mm(pst0[0:1, :], onesb[:, 0:1], zb_sl(c), c == 0, c == DC - 1,
                   waits=[W("v", v_xb)] + (guard(pst0) if c == 0 else []))
                pe_sq3 = mm(pst1[0:1, :], onesb[:, 0:1], hq_sl(c), c == 0,
                            c == DC - 1,
                            waits=[W("s", s_sq)] + (guard(pst1) if c == 0 else []),
                            inc="pe")
                rot_sq.done(c, "pe", pe_sq3)
                if c == 0:
                    v_M = vo(lambda e: e.tensor_copy(out=tA(0), in_=chunk(xf, 0)),
                             waits=rot_tA.wait(0), inc="v")
                    p_N = vo(lambda e: e.tensor_copy(out=tA(1), in_=chunk(xf, 0)),
                             waits=rot_tA.wait(1), inc="v")
                else:
                    v_M = vo(lambda e, c=c: e.tensor_tensor(
                        out=tA(0), in0=tA(0), in1=chunk(xf, c), op=OP.max),
                        inc="v")
                    p_N = vo(lambda e, c=c: e.tensor_tensor(
                        out=tA(1), in0=tA(1), in1=chunk(xf, c), op=OP.min),
                        inc="v")
                rot_xb.done(c, "pe", pe_sq3)
                rot_xb.done(c, "v", p_N)
            s_ln3, v_mub3 = ln_rows(pe_sq3, D)
            vs(lambda e: e.tensor_tensor(out=R(3), in0=R(2), in1=R(2), op=OP.mult))
            v_m3 = vs(lambda e: e.tensor_tensor(out=R(3), in0=R(3), in1=R(1),
                                                op=OP.mult))
            ss(lambda e: e.activation(out=R(3), in_=R(3), func=AF.Ln,
                                      bias=epsc[0:1, 0:1]), waits=[W("v", v_m3)])
            s_rms = ss(lambda e: e.activation(out=R(3), in_=R(3), func=AF.Exp,
                                              scale=-0.5))
            v_tm = tree_max_f32(tA(0), OP.max, 0, pre_w=[W("v", v_M)])
            v_tn = tree_max_f32(tA(1), OP.min, 1, pre_w=[W("v", p_N)])
            rot_tA.done(0, "v", v_tm)
            rot_tA.done(1, "v", v_tn)
            v_mc = row_to_cols(0, 8, pre_w=[W("s", s_rms)])
            vs(lambda e: e.tensor_tensor(out=tcol[:, 0:4], in0=tcol[:, 0:4],
                                         in1=tcol[:, 8:12], op=OP.subtract),
               waits=[W("v", v_mc)])
            vs(lambda e: e.tensor_tensor(out=tcol[:, 4:8], in0=tcol[:, 8:12],
                                         in1=tcol[:, 4:8], op=OP.subtract))
            v_am = vs(lambda e: e.tensor_tensor(out=tcol[:, 0:4], in0=tcol[:, 0:4],
                                                in1=tcol[:, 4:8], op=OP.max))
            cols_to_row(0, 4, pre_w=[W("v", v_am)])
            vs(lambda e: e.tensor_tensor(out=R(4), in0=R(4), in1=R(2), op=OP.mult))
            vs(lambda e: e.tensor_tensor(out=R(4), in0=R(4), in1=R(3), op=OP.mult))
            vs(lambda e: e.tensor_scalar(out=R(4), in0=R(4), scalar1=1e-5,
                                         scalar2=None, op0=OP.max))
            vs(lambda e: e.tensor_scalar(out=R(1), in0=R(4),
                                         scalar1=wrow[0:1, 4:5], scalar2=None,
                                         op0=OP.mult))
            vs(lambda e: e.reciprocal(out=R(4), in_=R(4)))
            vs(lambda e: e.tensor_scalar(out=R(4), in0=R(4), scalar1=127.0,
                                         scalar2=None, op0=OP.mult))
            vs(lambda e: e.tensor_tensor(out=R(2), in0=R(2), in1=R(3), op=OP.mult))
            vs(lambda e: e.tensor_tensor(out=R(2), in0=R(2), in1=R(4), op=OP.mult))
            v_sc = vs(lambda e: e.tensor_tensor(out=R(3), in0=R(0), in1=R(2),
                                                op=OP.mult))
            s_bA3 = bcast(R(2), pst0, bcA[:, :], False, row_w=[W("v", v_sc)])
            s_bB3 = bcast(R(3), pst1, bcB[:, :], False, row_w=[W("v", v_sc)])
            s_cb = 0
            for c in range(DC):
                v_t1 = vo(lambda e, c=c: e.scalar_tensor_tensor(
                    out=tA(c), in0=chunk(xf, c), scalar=1.0, in1=bcA[:, :],
                    op0=OP.bypass, op1=OP.mult),
                    waits=([W("s", s_bA3)] if c == 0 else []) + rot_tA.wait(c),
                    inc="v")
                p_t2 = po(lambda e, c=c: e.tensor_tensor(
                    out=tB(c), in0=tA(c), in1=bcB[:, 0:512], op=OP.subtract),
                    waits=[W("v", v_t1)] + ([W("s", s_bB3)] if c == 0 else []) +
                          rot_tB.wait(c), inc="p")
                rot_tA.done(c, "p", p_t2)
                v_ci = vo(lambda e, c=c: e.tensor_scalar(
                    out=aux(c), in0=tB(c), scalar1=-128.0, scalar2=127.0,
                    op0=OP.max, op1=OP.min),
                    waits=[W("p", p_t2)] + rot_aux.wait(c), inc="v")
                rot_tB.done(c, "v", v_ci)
                s_cb = so(lambda e, c=c: e.copy(out=chunk(yb, c), in_=aux(c)),
                          waits=[W("v", v_ci)] + (yb_free3 if c == 0 else []),
                          inc="s")
                rot_aux.done(c, "s", s_cb)
            if debug and i == 0:
                dma("gpsimd", dbgz[:], yb[:], waits=[W("s", s_cb)])
            s_d3 = bcast(R(1), pst0, bcA[:, :], False,
                         row_w=[W("v", sch.at("v"))])
            s_ge = {}
            pk = 0
            for oc2 in range(32):
                par = oc2 % 2
                c_w = dma("sync",
                          A2[:, par * 8192:par * 8192 + 4096].rearrange(
                              "p (c o) -> p c o", c=16),
                          wq1d[:, oc2 * 256:(oc2 + 1) * 256].rearrange(
                              "(kc p) o -> p kc o", p=128),
                          waits=rot_w1.wait(oc2) +
                                ([W("s", sch.at("s"))] if i == 0 and oc2 < 2
                                 else []))
                for oi in range(2):
                    oc = oc2 * 2 + oi
                    g = oc % 2
                    for kc in range(DC):
                        pk = mm(pmm[g][:, 0:512],
                                A2[:, par * 8192 + kc * 256 + oi * 128:
                                   par * 8192 + kc * 256 + oi * 128 + 128],
                                chunk(yb, kc), kc == 0, kc == DC - 1,
                                waits=([W("ld", c_w), W("s", s_cb)]
                                       if kc == 0 else []) +
                                      (guard(pmm[g]) if kc == 0 else []),
                                inc="pe" if kc == DC - 1 else None)
                    v_ds = vo(lambda e, g=g, oc=oc: e.scalar_tensor_tensor(
                        out=tA(oc), in0=pmm[g][:, 0:512], scalar=1.0,
                        in1=bcA[:, :], op0=OP.bypass, op1=OP.mult),
                        waits=[W("pe", pk)] + ([W("s", s_d3)] if oc == 0 else []) +
                              rot_tA.wait(oc), inc="v")
                    consumed(pmm[g], "v", v_ds)
                    s_si = so(lambda e, oc=oc: e.activation(
                        out=chunk(A1, oc), in_=tA(oc), func=AF.Silu,
                        bias=b1c[:, oc:oc + 1]),
                        waits=[W("v", v_ds)], inc="s")
                    rot_tA.done(oc, "s", s_si)
                rot_w1.done(oc2, "pe", pk)
                if oc2 % 8 == 7:
                    base = (oc2 // 8) * 16
                    for m in range(base, base + 16):
                        s_ge[m] = so(lambda e, m=m: e.activation(
                            out=chunk(A1, m), in_=chunk(A1, m), func=AF.Gelu),
                            inc="s")
            if debug and i == 0:
                dma("gpsimd", dbgh[:], A1[:], waits=[W("s", sch.at("s"))])
            pe_sq4 = 0
            for m in range(MC):
                s_hq = so(lambda e, m=m: e.activation(
                    out=hq_sl(m), in_=chunk(A1, m), func=AF.Square),
                    waits=rot_sq.wait(m), inc="s")
                mm(pst0[0:1, :], onesb[:, 0:1], chunk(A1, m), m == 0, m == MC - 1,
                   waits=[W("s", s_ge[m])] + (guard(pst0) if m == 0 else []))
                pe_sq4 = mm(pst1[0:1, :], onesb[:, 0:1], hq_sl(m), m == 0,
                            m == MC - 1,
                            waits=[W("s", s_hq)] + (guard(pst1) if m == 0 else []),
                            inc="pe")
                rot_sq.done(m, "pe", pe_sq4)
            s_ln4, v_mr4 = ln_rows(pe_sq4, HID)
            s_bA4 = bcast(R(2), pst0, bcAb[:, :], True, row_w=[W("v", v_mr4)])
            s_bB4 = bcast(R(3), pst1, bcB[:, :], False, row_w=[W("v", v_mr4)])
            v_l3s = {}
            for m in range(MC):
                v_l1 = vo(lambda e, m=m: e.scalar_tensor_tensor(
                    out=tA(m), in0=chunk(A1, m), scalar=1.0, in1=bcAb[:, :],
                    op0=OP.bypass, op1=OP.mult),
                    waits=([W("s", s_bA4)] if m == 0 else []) + rot_tA.wait(m),
                    inc="v")
                p_l2 = po(lambda e, m=m: e.tensor_tensor(
                    out=tB(m), in0=tA(m), in1=bcB[:, 0:512], op=OP.subtract),
                    waits=[W("v", v_l1)] + ([W("s", s_bB4)] if m == 0 else []) +
                          rot_tB.wait(m), inc="p")
                rot_tA.done(m, "p", p_l2)
                v_l3s[m] = vo(lambda e, m=m: e.tensor_scalar(
                    out=chunk(A1, m), in0=tB(m), scalar1=gc[:, m:m + 1],
                    scalar2=bbc[:, m:m + 1], op0=OP.mult, op1=OP.add),
                    waits=[W("p", p_l2)], inc="v")
                rot_tB.done(m, "v", v_l3s[m])
                p_l2_last = p_l2
            if debug and i == 0:
                dma("gpsimd", dbgl[:], A1[:], waits=[W("v", v_l3s[MC - 1])])
            pe_q5 = 0
            vp_mx = ("v", 0)
            for m in range(MC):
                s_q5 = so(lambda e, m=m: e.activation(
                    out=hq_sl(m), in_=chunk(A1, m), func=AF.Square),
                    waits=[W("v", v_l3s[m])] + rot_sq.wait(m), inc="s")
                pe_q5 = mm(pst0[0:1, :], onesb[:, 0:1], hq_sl(m), m == 0,
                           m == MC - 1,
                           waits=[W("s", s_q5)] + (guard(pst0) if m == 0 else []),
                           inc="pe")
                if m == 0:
                    vo(lambda e: e.tensor_copy(out=MQpp(0), in_=chunk(A1, 0)))
                    mx = vo(lambda e: e.tensor_copy(out=Npp(0), in_=chunk(A1, 0)),
                            inc="v")
                else:
                    vo(lambda e, m=m: e.tensor_tensor(
                        out=MQpp(m), in0=MQpp(m - 1), in1=chunk(A1, m),
                        op=OP.max))
                    mx = vo(lambda e, m=m: e.tensor_tensor(
                        out=Npp(m), in0=Npp(m - 1), in1=chunk(A1, m),
                        op=OP.min), inc="v")
                vp_mx = ("v", mx)
                rot_sq.done(m, "pe", pe_q5)
            s_m5 = ss(lambda e: e.activation(out=R(0), in_=pst0[0:1, :],
                                             func=AF.Identity, scale=1.0 / HID),
                      waits=[W("pe", pe_q5)])
            consumed(pst0, "s", s_m5)
            ss(lambda e: e.activation(out=R(1), in_=R(0), func=AF.Ln,
                                      bias=epsc[0:1, 0:1]))
            s_r5 = ss(lambda e: e.activation(out=R(1), in_=R(1), func=AF.Exp,
                                             scale=-0.5))
            v_t5 = tree_max(MQpp(MC - 1), OP.max, 0, pre_w=[W(*vp_mx)])
            v_t6 = tree_max(Npp(MC - 1), OP.min, 1, pre_w=[W("v", v_t5)])
            vs(lambda e: e.tensor_scalar(out=tcol[:, 4:8], in0=tcol[:, 4:8],
                                         scalar1=-1.0, scalar2=None, op0=OP.mult),
               waits=[W("v", v_t6)])
            v_am5 = vs(lambda e: e.tensor_tensor(out=tcol[:, 0:4],
                                                 in0=tcol[:, 0:4],
                                                 in1=tcol[:, 4:8], op=OP.max))
            v_msq = cols_to_row(0, 2, pre_w=[W("v", v_am5)])
            vs(lambda e: e.tensor_tensor(out=R(2), in0=R(2), in1=R(1), op=OP.mult),
               waits=[W("v", v_msq), W("s", s_r5)])
            vs(lambda e: e.tensor_scalar(out=R(2), in0=R(2), scalar1=1e-5,
                                         scalar2=None, op0=OP.max))
            vs(lambda e: e.tensor_scalar(out=R(4), in0=R(2),
                                         scalar1=wrow[0:1, 5:6], scalar2=None,
                                         op0=OP.mult))
            vs(lambda e: e.reciprocal(out=R(2), in_=R(2)))
            vs(lambda e: e.tensor_scalar(out=R(2), in0=R(2), scalar1=127.0,
                                         scalar2=None, op0=OP.mult))
            v_sc5 = vs(lambda e: e.tensor_tensor(out=R(2), in0=R(2), in1=R(1),
                                                 op=OP.mult))
            s_bA5 = bcast(R(2), pst0, bcAb[:, :], True, row_w=[W("v", v_sc5)])
            s_bB5 = bcast(R(4), pst1, bcB[:, :], False,
                          row_w=[W("v", v_sc5), W("p", p_l2_last)])
            s_z3 = 0
            for m in range(MC):
                v_z1 = vo(lambda e, m=m: e.scalar_tensor_tensor(
                    out=tA(m), in0=chunk(A1, m), scalar=1.0, in1=bcAb[:, :],
                    op0=OP.bypass, op1=OP.mult),
                    waits=([W("s", s_bA5)] if m == 0 else []) + rot_tA.wait(m),
                    inc="v")
                v_z2 = vo(lambda e, m=m: e.tensor_scalar(
                    out=aux(m), in0=tA(m), scalar1=-128.0, scalar2=127.0,
                    op0=OP.max, op1=OP.min),
                    waits=rot_aux.wait(m), inc="v")
                rot_tA.done(m, "v", v_z2)
                s_z3 = so(lambda e, m=m: e.copy(out=chunk(A1, m), in_=aux(m)),
                          waits=[W("v", v_z2)], inc="s")
                rot_aux.done(m, "s", s_z3)
            if debug and i == 0:
                dma("gpsimd", dbgq[:], A1[:], waits=[W("s", s_z3)])
            pe_sq6 = 0
            rot_sq6 = Rot(2)
            for jc in range(DC):
                par = jc % 2
                c_w2 = dma("sync",
                           A2[:, par * 8192:(par + 1) * 8192].rearrange(
                               "p (c j) -> p c j", c=64),
                           wq2d[jc].rearrange("(kc p) j -> p kc j", p=128),
                           waits=rot_w2.wait(jc))
                g = jc % 2
                pk2 = None
                for kc in range(MC):
                    pk2 = mm(pmm[g][:, 0:512],
                             A2[:, par * 8192 + kc * 128:par * 8192 + kc * 128 + 128],
                             chunk(A1, kc), kc == 0, kc == MC - 1,
                             waits=([W("ld", c_w2)] if kc == 0 else []) +
                                   ([W("s", s_z3)] if jc == 0 and kc == 0 else []) +
                                   (guard(pmm[g]) if kc == 0 else []),
                             inc="pe" if kc == MC - 1 else None)
                rot_w2.done(jc, "pe", pk2)
                v_d5 = vo(lambda e, g=g, jc=jc: e.scalar_tensor_tensor(
                    out=tA(jc), in0=pmm[g][:, 0:512], scalar=1.0, in1=bcB[:, :],
                    op0=OP.bypass, op1=OP.mult),
                    waits=([W("pe", pk2), W("s", s_bB5)] if jc == 0
                           else [W("pe", pk2)]) + rot_tA.wait(jc),
                    inc="v")
                consumed(pmm[g], "v", v_d5)
                v_b2 = vo(lambda e, jc=jc: e.tensor_scalar(
                    out=chunk(xb1, jc), in0=tA(jc), scalar1=b2c[:, jc:jc + 1],
                    scalar2=None, op0=OP.add),
                    waits=(xb1_free3 if jc == 0 else []), inc="v")
                rot_tA.done(jc, "v", v_b2)
                s_q6 = so(lambda e, jc=jc: e.activation(
                    out=hq_sl(jc), in_=chunk(xb1, jc), func=AF.Square),
                    waits=[W("v", v_b2)] + rot_sq6.wait(jc), inc="s")
                mm(pst0[0:1, :], onesb[:, 0:1], chunk(xb1, jc), jc == 0,
                   jc == DC - 1,
                   waits=[W("v", v_b2)] + (guard(pst0) if jc == 0 else []))
                pe_sq6 = mm(pst1[0:1, :], onesb[:, 0:1], hq_sl(jc), jc == 0,
                            jc == DC - 1,
                            waits=[W("s", s_q6)] + (guard(pst1) if jc == 0 else []),
                            inc="pe")
                rot_sq6.done(jc, "pe", pe_sq6)
            if debug and i == 0:
                dma("gpsimd", dbga[:], xb1[:], waits=[W("pe", pe_sq6)])
            s_ln6, v_mub6 = ln_rows(pe_sq6, D)
            s_bA6 = bcast(R(2), pst0, bcAb[:, :], True, row_w=[W("v", v_mub6)])
            s_bB6 = bcast(R(3), pst1, bcB[:, :], False, row_w=[W("v", v_mub6)])
            v_res = p_res = 0
            for c in range(DC):
                v_t1 = vo(lambda e, c=c: e.scalar_tensor_tensor(
                    out=tA(c), in0=chunk(xb1, c), scalar=1.0, in1=bcAb[:, :],
                    op0=OP.bypass, op1=OP.mult),
                    waits=([W("s", s_bA6)] if c == 0 else []) + rot_tA.wait(c),
                    inc="v")
                p_res = po(lambda e, c=c: e.tensor_tensor(
                    out=tB(c), in0=tA(c), in1=bcB[:, 0:512], op=OP.subtract),
                    waits=[W("v", v_t1)] + ([W("s", s_bB6)] if c == 0 else []) +
                          rot_tB.wait(c), inc="p")
                rot_tA.done(c, "p", p_res)
                v_res = vo(lambda e, c=c: e.tensor_tensor(
                    out=chunk(xf, c), in0=chunk(xf, c), in1=tB(c), op=OP.add),
                    waits=[W("p", p_res)], inc="v")
                rot_tB.done(c, "v", v_res)
            st_out = dma("scalar",
                         outT[:, i * T:(i + 1) * T].rearrange("(c p) t -> p c t",
                                                              p=128),
                         xf[:].rearrange("p (c t) -> p c t", c=DC),
                         waits=[W("v", v_res)])
            xf_free = [W("st", st_out)]
            sqs_free3 = [W("pe", pe_sq6)]
            yb_free3 = [W("pe", sch.at("pe"))]
            xb1_free3 = [W("v", v_res), W("p", p_res)]

        final_st = sch.at("st")
        final_stg = sch.at("stg")
        sch.op("sync", None, waits=[W("st", final_st), W("stg", final_stg)])

        def make_body(eng_name):
            def body(e):
                for ws, fn, inc, amt in sch.q[eng_name]:
                    for (sem, val) in ws:
                        e.wait_ge(sem, val)
                    if fn is not None:
                        inst = fn(e)
                        if inc is not None and inst is not None:
                            inst.then_inc(inc, amt)
            return body

        block = es.enter_context(nc.Block())
        block.sync(make_body("sync"))
        block.tensor(make_body("tensor"))
        block.vector(make_body("vector"))
        block.scalar(make_body("scalar"))
        block.gpsimd(make_body("gpsimd"))

    return nc


_NC_CACHE = {}


def kernel(L0_attn_hidden_states, L0_attn_Wq, L0_attn_bq, L0_attn_Wk, L0_attn_bk,
           L0_attn_Wv, L0_attn_bv, L0_attn_Wo, L0_ffn_w1, L0_ffn_b1, L0_ffn_ln_g,
           L0_ffn_ln_b, L0_ffn_w2, L0_ffn_b2, _debug=False):
    import ml_dtypes
    x = np.asarray(L0_attn_hidden_states, dtype=np.float32)
    C = lambda a: np.ascontiguousarray(a, dtype=np.float32)
    shared = {
        "wqT": C(np.asarray(L0_attn_Wq).T),
        "wkT": C(np.asarray(L0_attn_Wk).T),
        "wvT": C(np.asarray(L0_attn_Wv).T),
        "woT": C(np.asarray(L0_attn_Wo).T),
        "w1T": C(np.asarray(L0_ffn_w1).T),
        "w2T": C(np.asarray(L0_ffn_w2).T),
        "bq_r": C(np.asarray(L0_attn_bq).reshape(DC, 128).T),
        "bk_r": C(np.asarray(L0_attn_bk).reshape(1, 512)),
        "bv_r": C(np.asarray(L0_attn_bv).reshape(1, 512)),
        "b1_r": C(np.asarray(L0_ffn_b1).reshape(MC, 128).T),
        "b2_r": C(np.asarray(L0_ffn_b2).reshape(DC, 128).T),
        "g_r": C(np.asarray(L0_ffn_ln_g).reshape(MC, 128).T),
        "bb_r": C(np.asarray(L0_ffn_ln_b).reshape(MC, 128).T),
        "identb": np.ascontiguousarray(np.eye(128, dtype=ml_dtypes.bfloat16)),
        "identf": np.ascontiguousarray(np.eye(128, dtype=np.float32)),
    }
    in_maps = []
    for c in range(NCORES):
        m = dict(shared)
        m["xT"] = C(x[c].T)
        in_maps.append(m)
    key = ("v1", _debug)
    if key not in _NC_CACHE:
        _NC_CACHE[key] = build_nc(debug=_debug)
    nc = _NC_CACHE[key]
    res = run_bass_kernel_spmd(nc, in_maps, core_ids=list(range(NCORES)))
    out = np.stack([np.ascontiguousarray(r["outT"].T) for r in res.results], axis=0)
    if _debug:
        return out, res
    return out
